# version 03
# baseline (speedup 1.0000x reference)
import numpy as np
from contextlib import ExitStack
import concourse.bass as bass
import concourse.mybir as mybir
from concourse.bass_utils import run_bass_kernel_spmd

F32 = mybir.dt.float32
BF16 = mybir.dt.bfloat16
I32 = mybir.dt.int32
AF = mybir.ActivationFunctionType
ALU = mybir.AluOpType
AXX = mybir.AxisListType.X

NCORE = 8
T = 8192
D = 2048
TC = T // NCORE
NCH = D // 128
IFF = 8192
NIC = IFF // 128
EPS = float(1e-5)
NEG = float(-1e30)

ENGS = ("sync", "gpsimd", "tensor", "vector", "scalar")
ESEM = {"sync": "sY", "gpsimd": "sG", "tensor": "sT", "vector": "sV", "scalar": "sS"}


class P:

    NRING = 8

    def __init__(self, nc):
        self.nc = nc
        self.es = ExitStack()
        self.q = {e: [] for e in ENGS}
        self.cnt = {}
        self.sems = {}
        for s in ("sY", "sG", "sT", "sV", "sS", "sC"):
            self.sems[s] = self.es.enter_context(nc.semaphore(s))
            self.cnt[s] = 0
        self.ring = {}
        self.ring_idx = {}
        for qn in ("sY", "sG"):
            self.ring[qn] = []
            for i in range(self.NRING):
                name = f"{qn}r{i}"
                self.sems[name] = self.es.enter_context(nc.semaphore(name))
                self.cnt[name] = 0
                self.ring[qn].append(name)
            self.ring_idx[qn] = 0
        self._lastwait = {}

    def do(self, eng, fn, waits=(), sem=None, by=1):
        wl = []
        for tok in waits:
            if tok is None:
                continue
            sn, v = tok
            if v <= 0:
                continue
            key = (eng, sn)
            if self._lastwait.get(key, -1) >= v:
                continue
            self._lastwait[key] = v
            wl.append((sn, v))
        sn_inc = sem if sem is not None else ESEM[eng]
        sems = self.sems
        if by == 16 and sn_inc in ("sY", "sG"):
            slot = self.ring_idx[sn_inc] % self.NRING
            self.ring_idx[sn_inc] += 1
            rname = self.ring[sn_inc][slot]
            prev = self.cnt[rname]
            if prev > 0:
                key = (eng, rname)
                if self._lastwait.get(key, -1) < prev:
                    self._lastwait[key] = prev
                    wl.insert(0, (rname, prev))
            self.cnt[rname] = prev + 16
            val = self.cnt[rname]

            def emitd(engine, fn=fn, wl=tuple(wl), rname=rname):
                for (sn, v) in wl:
                    engine.wait_ge(sems[sn], v)
                ins = fn(engine)
                assert ins is not None
                ins.then_inc(sems[rname], 16)
            self.q[eng].append(emitd)
            return (rname, val)
        self.cnt[sn_inc] += by
        val = self.cnt[sn_inc]

        def emit(engine, fn=fn, wl=tuple(wl), sn_inc=sn_inc, by=by):
            for (sn, v) in wl:
                engine.wait_ge(sems[sn], v)
            ins = fn(engine)
            assert ins is not None, "op fn must return the BassInstruction"
            ins.then_inc(sems[sn_inc], by)
        self.q[eng].append(emit)
        return (sn_inc, val)

    def wait_only(self, eng, waits):
        wl = []
        for tok in waits:
            if tok is None:
                continue
            sn, v = tok
            key = (eng, sn)
            if self._lastwait.get(key, -1) >= v:
                continue
            self._lastwait[key] = v
            wl.append((sn, v))
        if not wl:
            return
        sems = self.sems

        def emit(engine, wl=tuple(wl)):
            for (sn, v) in wl:
                engine.wait_ge(sems[sn], v)
        self.q[eng].append(emit)

    def now(self, sn):
        return (sn, self.cnt[sn])

    def dma_drain_toks(self):
        return [(r, self.cnt[r]) for qn in ("sY", "sG") for r in self.ring[qn]
                if self.cnt[r] > 0]

    def full_barrier(self):
        names = ["sY", "sG", "sT", "sV", "sS", "sC"]
        names += [r for qn in ("sY", "sG") for r in self.ring[qn]]
        toks = [(sn, self.cnt[sn]) for sn in names]
        for eng in ENGS:
            self.wait_only(eng, toks)

    def finalize(self):
        nc = self.nc
        q = self.q
        with nc.Block() as block:
            @block.sync
            def _(e):
                for f in q["sync"]:
                    f(e)

            @block.gpsimd
            def _(e):
                for f in q["gpsimd"]:
                    f(e)

            @block.tensor
            def _(e):
                for f in q["tensor"]:
                    f(e)

            @block.vector
            def _(e):
                for f in q["vector"]:
                    f(e)

            @block.scalar
            def _(e):
                for f in q["scalar"]:
                    f(e)
        self.es.close()


class Ctx:
    pass


def alloc_common(p, es):
    nc = p.nc
    c = Ctx()
    c.ones_col_bf = es.enter_context(nc.sbuf_tensor("c_ones_col", [128, 1], BF16))
    c.ones_row_bf = es.enter_context(nc.sbuf_tensor("c_ones_rowb", [1, 128], BF16))
    c.ones_row_f = es.enter_context(nc.sbuf_tensor("c_ones_rowf", [1, 128], F32))
    c.ones_row512_f = es.enter_context(nc.sbuf_tensor("c_ones_rowf5", [1, 512], F32))
    c.gpwarm = es.enter_context(nc.sbuf_tensor("c_gpwarm", [1, 16], F32))
    gw0 = p.do("gpsimd", lambda e: e.dma_start(out=c.gpwarm[0:1, 0:8],
                                               in_=c.gpwarm[0:1, 8:16]),
               sem="sG", by=16)
    t0 = p.do("vector", lambda e: e.memset(c.ones_col_bf[:], 1.0))
    t1 = p.do("vector", lambda e: e.memset(c.ones_row_bf[:], 1.0), waits=[t0])
    t2 = p.do("vector", lambda e: e.memset(c.ones_row512_f[:], 1.0), waits=[t1])
    c.consts = p.do("vector", lambda e: e.memset(c.ones_row_f[:], 1.0), waits=[t2])
    c.x = es.enter_context(nc.sbuf_tensor("x_res", [128, NCH * TC], F32))
    return c


def fmview(dram_ap):
    return dram_ap.rearrange("(k p) t -> p k t", p=128)


def xsbview(x_sb):
    return x_sb[:, :].rearrange("p (k t) -> p k t", k=NCH)


def wsv(sb):
    return sb[:, :].rearrange("p (kc m) -> p kc m", m=128)


def load_x(p, c, x_fm_ap):
    c.x_tok = p.do("sync", lambda e: e.dma_start(out=xsbview(c.x), in_=fmview(x_fm_ap)),
                   by=16)


def emit_ln_prescale(p, es, c, xp, x_dep, tag, l1_combined=False,
                     gamma_col=None, beta_col=None, h_out=None):
    if x_dep is None:
        x_dep = []
    elif isinstance(x_dep, tuple) and len(x_dep) == 2 and isinstance(x_dep[0], str):
        x_dep = [x_dep]
    else:
        x_dep = list(x_dep)
    """stats over c.x; xp = bf16((x-mu)*a). If l1_combined, h_out = bf16
    (gamma*((x-mu)*a1*a2)+beta) and xp = (x-mu)*a1 stays UNUSED (pass xp=None);
    returns (rows_sb, deps) where rows holds mu at [0:TC], a1 at [TC:2TC]."""
    nc = p.nc
    ph = ExitStack()
    sq = ph.enter_context(nc.sbuf_tensor(f"ln_sq{tag}", [128, 2 * TC], BF16, side="right"))
    xb = ph.enter_context(nc.sbuf_tensor(f"ln_xb{tag}", [128, 2 * TC], BF16, side="right"))
    ps_row = ph.enter_context(nc.psum_tensor(f"ln_psrow{tag}", [1, 2048], F32, side="right"))
    ps_bc = ph.enter_context(nc.psum_tensor(f"ln_psbc{tag}", [128, 2 * TC], F32, side="right"))
    rows_owner = es if l1_combined else ph
    rows = rows_owner.enter_context(nc.sbuf_tensor(f"ln_rows{tag}", [1, 8 * TC], F32, side="left" if l1_combined else "right"))
    mb = ph.enter_context(nc.sbuf_tensor(f"ln_mb{tag}", [128, TC], F32, side="right"))
    ab = ph.enter_context(nc.sbuf_tensor(f"ln_ab{tag}", [128, TC], F32, side="right"))
    t1 = ph.enter_context(nc.sbuf_tensor(f"ln_t1{tag}", [128, 2 * TC], F32, side="right"))

    mm_last = [None, None]
    for k in range(NCH):
        seg = slice(k * TC, (k + 1) * TC)
        kb = k % 2
        bseg = slice(kb * TC, (kb + 1) * TC)
        w = x_dep + [mm_last[kb]]
        vS = p.do("scalar", lambda e, seg=seg, bseg=bseg:
                  e.activation(xb[:, bseg], c.x[:, seg], AF.Copy), waits=w)
        vV = p.do("vector", lambda e, seg=seg, bseg=bseg:
                  e.tensor_mul(sq[:, bseg], c.x[:, seg], c.x[:, seg]), waits=w)
        for tg in range(TC // 512):
            ts = slice(kb * TC + tg * 512, kb * TC + tg * 512 + 512)
            p.do("tensor", lambda e, ts=ts, tg=tg, k=k:
                 e.matmul(ps_row[0:1, tg * 512:(tg + 1) * 512],
                          c.ones_col_bf[:, 0:1], xb[:, ts],
                          start=(k == 0), stop=(k == NCH - 1)),
                 waits=[vS, c.consts])
            mm_last[kb] = p.do("tensor", lambda e, ts=ts, tg=tg, k=k:
                               e.matmul(ps_row[0:1, 1024 + tg * 512:1024 + (tg + 1) * 512],
                                        c.ones_col_bf[:, 0:1], sq[:, ts],
                                        start=(k == 0), stop=(k == NCH - 1)),
                               waits=[vV])
    mm_done = p.now("sT")

    R = rows

    def rs(i):
        return R[0:1, i * TC:(i + 1) * TC]
    v = p.do("vector", lambda e: e.tensor_scalar_mul(rs(0), ps_row[0:1, 0:1024], 1.0 / D),
             waits=[mm_done])
    v = p.do("vector", lambda e: e.tensor_mul(rs(3), rs(0), rs(0)), waits=[v])
    v = p.do("vector", lambda e: e.scalar_tensor_tensor(
        rs(2), ps_row[0:1, 1024:2048], 1.0 / D, rs(3), ALU.mult, ALU.subtract),
        waits=[v])
    v = p.do("vector", lambda e: e.tensor_scalar_add(rs(3), rs(2), EPS), waits=[v])
    v = p.do("vector", lambda e: e.reciprocal(rs(4), rs(3)), waits=[v])
    s = p.do("scalar", lambda e: e.activation(rs(1), rs(4), AF.Sqrt), waits=[v])
    a_done = s
    if l1_combined:
        s = p.do("vector", lambda e: e.tensor_copy(rs(5), rs(1)), waits=[s])
        v = p.do("vector", lambda e: e.reciprocal(rs(4), rs(3)), waits=[s])
        v = p.do("vector", lambda e: e.tensor_mul(rs(4), rs(2), rs(4)),
                 waits=[v])
        v = p.do("vector", lambda e: e.tensor_scalar_add(rs(4), rs(4), EPS), waits=[v])
        v = p.do("vector", lambda e: e.reciprocal(rs(3), rs(4)), waits=[v])
        s = p.do("scalar", lambda e: e.activation(rs(3), rs(3), AF.Sqrt), waits=[v])
        v = p.do("vector", lambda e: e.tensor_mul(rs(1), rs(5), rs(3)), waits=[s])
        a_done = v

    bc = None
    for tg in range(TC // 512):
        ts = slice(tg * 512, (tg + 1) * 512)
        p.do("tensor", lambda e, ts=ts: e.matmul(
            ps_bc[:, ts], c.ones_row_f[0:1, :], R[0:1, ts], start=True, stop=True),
            waits=[a_done, c.consts])
        bc = p.do("tensor", lambda e, tg=tg: e.matmul(
            ps_bc[:, TC + tg * 512:TC + (tg + 1) * 512], c.ones_row_f[0:1, :],
            R[0:1, TC + tg * 512:TC + (tg + 1) * 512], start=True, stop=True))
    v1 = p.do("vector", lambda e: e.tensor_copy(mb[:, :], ps_bc[:, 0:TC]), waits=[bc])
    v2 = p.do("vector", lambda e: e.tensor_copy(ab[:, :], ps_bc[:, TC:2 * TC]), waits=[v1])

    out_deps = []
    vprev = [v2, v2]
    for k in range(NCH):
        seg = slice(k * TC, (k + 1) * TC)
        kb = k % 2
        bseg = slice(kb * TC, (kb + 1) * TC)
        va = p.do("vector", lambda e, seg=seg, bseg=bseg:
                  e.tensor_sub(t1[:, bseg], c.x[:, seg], mb[:, :]),
                  waits=[vprev[kb], v2] + x_dep)
        tgt = xp if not l1_combined else sq
        if not l1_combined:
            vbf = p.do("vector", lambda e, seg=seg, bseg=bseg:
                       e.tensor_mul(xp[:, seg], t1[:, bseg], ab[:, :]), waits=[va])
            out_deps.append(vbf)
            vprev[kb] = vbf
        else:
            vz = p.do("vector", lambda e, seg=seg, bseg=bseg:
                      e.tensor_mul(t1[:, bseg], t1[:, bseg], ab[:, :]), waits=[va])
            sh = p.do("scalar", lambda e, seg=seg, bseg=bseg, k=k:
                      e.activation(h_out[:, seg], t1[:, bseg], AF.Identity,
                                   bias=beta_col[:, k:k + 1],
                                   scale=gamma_col[:, k:k + 1]), waits=[vz])
            out_deps.append(sh)
            vprev[kb] = vz
    ph.close()
    return rows, out_deps


def emit_proj(p, es, c, tag, wT_dram, Ktot, oc_list, rhs_sb, rhs_seg_stride,
              rhs_deps, evac_cb, w_dtype_cast=True, tgs=2, tgw=512,
              extra_mm_cb=None):
    nc = p.nc
    NKC = Ktot // 128
    ph = ExitStack()
    wbuf = [ph.enter_context(nc.sbuf_tensor(f"w_{tag}{i}", [128, NKC * 128], BF16, side="right"))
            for i in range(2)]
    ps = ph.enter_context(nc.psum_tensor(f"ps_{tag}", [128, 2 * tgw], F32, side="right"))
    dma_eng = "gpsimd" if w_dtype_cast else "sync"
    dma_sem = "sG" if w_dtype_cast else "sY"
    slab_tok = []
    consumed = [None, None]
    evac_toks = []
    for oi, oc in enumerate(oc_list):
        bi = oi % 2
        wv = wT_dram.rearrange("(kc p) m -> p kc m", p=128)[
            :, :].tensor if False else None
        def mk_dma(oc=oc, bi=bi):
            def f(e):
                src = wT_dram[:, oc * 128:(oc + 1) * 128].rearrange(
                    "(kc p) m -> p kc m", p=128)
                return e.dma_start(out=wsv(wbuf[bi]), in_=src)
            return f
        tok = p.do(dma_eng, mk_dma(), waits=[consumed[bi]], sem=dma_sem, by=16)
        slab_tok.append(tok)
        for tg in range(tgs):
            pslc = ps[:, tg * tgw:(tg + 1) * tgw]
            mm = None
            for k in range(NKC):
                rseg = slice(k * rhs_seg_stride + tg * tgw,
                             k * rhs_seg_stride + tg * tgw + tgw)
                w = [tok, evac_toks[-4] if len(evac_toks) >= 4 else None]
                if k == 0:
                    w += rhs_deps
                mm = p.do("tensor", lambda e, k=k, rseg=rseg, pslc=pslc, bi=bi:
                          e.matmul(pslc, wbuf[bi][:, k * 128:(k + 1) * 128],
                                   rhs_sb[:, rseg], start=(k == 0),
                                   stop=(k == NKC - 1 and extra_mm_cb is None)),
                          waits=w)
            if extra_mm_cb is not None:
                mm = extra_mm_cb(oi, oc, tg, pslc)
            ev = evac_cb(oi, oc, tg, pslc, mm)
            evac_toks.append(ev)
        consumed[bi] = mm
    ph.close()
    return evac_toks


def emit_ffn(p, es, c, tag, xp, xp_deps, quant, gwT, uwT, dwT,
             gs_col=None, us_col=None, dshalf_col=None, h128_sb=None, dbg=None):
    nc = p.nc
    IBC = 8
    NIB = NIC // IBC
    ph = ExitStack()
    wg = [ph.enter_context(nc.sbuf_tensor(f"ffn_wg{tag}{i}", [128, NCH * 128], BF16))
          for i in range(2)]
    wu = [ph.enter_context(nc.sbuf_tensor(f"ffn_wu{tag}{i}", [128, NCH * 128], BF16))
          for i in range(2)]
    wd = [ph.enter_context(nc.sbuf_tensor(f"ffn_wd{tag}{i}", [128, IBC * 128], BF16))
          for i in range(2)]
    hb = [ph.enter_context(nc.sbuf_tensor(f"ffn_h{tag}{i}", [128, IBC * 512], BF16, side="right"))
          for i in range(2)]
    sg = [ph.enter_context(nc.sbuf_tensor(f"ffn_sg{tag}{i}", [128, 512], F32, side="right"))
          for i in range(2)]
    acc = ph.enter_context(nc.sbuf_tensor(f"ffn_acc{tag}", [128, NCH * 512], F32, side="right"))
    t2 = ph.enter_context(nc.sbuf_tensor(f"ffn_t2{tag}", [128, 2 * 512], F32, side="right"))
    ps_ab = ph.enter_context(nc.psum_tensor(f"ffn_psab{tag}", [128, 4 * 512], F32, side="right"))
    dbg_sb = ph.enter_context(nc.sbuf_tensor(f"ffn_dbgsb{tag}", [128, 512], F32, side="right")) if dbg is not None else None
    dbg_sb2 = ph.enter_context(nc.sbuf_tensor(f"ffn_dbgsb2{tag}", [128, 512], F32, side="right")) if dbg is not None else None
    ps_c = ph.enter_context(nc.psum_tensor(f"ffn_psc{tag}", [128, 2 * 512], F32, side="right"))
    if quant:
        ps_hsum = ph.enter_context(nc.psum_tensor(f"ffn_pshs{tag}", [1, 512], F32, side="right"))
        ps_d = ph.enter_context(nc.psum_tensor(f"ffn_psd{tag}", [128, 512], F32, side="right"))
        hsum_sb = ph.enter_context(nc.sbuf_tensor(f"ffn_hsum{tag}", [1, 512], F32, side="right"))
    else:
        ps_h = ph.enter_context(nc.psum_tensor(f"ffn_psh{tag}", [128, 2 * 512], F32, side="right"))
        h2b = [ph.enter_context(nc.sbuf_tensor(f"ffn_h2{tag}{i}", [128, IBC * 512], BF16, side="right"))
               for i in range(2)]

    w_cast_eng, w_cast_sem = ("gpsimd", "sG")
    dwe, dws = (w_cast_eng, w_cast_sem) if quant else ("sync", "sY")
    wv = p.do("vector", lambda e: e.memset(sg[0][:, 0:16], 0.0))
    ws = p.do("scalar", lambda e: e.activation(sg[0][:, 0:16], sg[0][:, 0:16],
                                               AF.Silu), waits=[wv])
    p.wait_only("scalar", [ws])
    final_toks = []
    hsum_prev_evac = None
    psd_prev_use = None
    acc_done = [None] * NCH
    gu_cons = [None, None]
    d_cons = [None, None]
    silu_done = [None, None]
    hstt_done = [None, None]
    h_used = [None, None]
    h2_done = [None, None]
    psC_guard = [None, None]
    for tg in range(TC // 512):
        hsum_tok = None
        for ib in range(NIB):
            hbuf = hb[ib % 2]
            h2buf = h2b[ib % 2] if not quant else None
            for j in range(IBC):
                ic = ib * IBC + j
                bi = ic % 2
                tku = p.do(w_cast_eng, lambda e, ic=ic, bi=bi: e.dma_start(
                    out=wu[bi][:, :], in_=uwT[ic * 128:(ic + 1) * 128, :]),
                    waits=[gu_cons[bi]], sem=w_cast_sem, by=16)
                tkg = p.do(w_cast_eng, lambda e, ic=ic, bi=bi: e.dma_start(
                    out=wg[bi][:, :], in_=gwT[ic * 128:(ic + 1) * 128, :]),
                    waits=[gu_cons[bi]], sem=w_cast_sem, by=16)
                psB = ps_ab[:, (bi * 2) * 512:(bi * 2 + 1) * 512]
                psA = ps_ab[:, (bi * 2 + 1) * 512:(bi * 2 + 2) * 512]
                mmg = None
                for k in range(NCH):
                    w = [tkg, silu_done[bi]] + (xp_deps if k == 0 else [])
                    mmg = p.do("tensor", lambda e, k=k, psA=psA, bi=bi, tg=tg:
                               e.matmul(psA, wg[bi][:, k * 128:(k + 1) * 128],
                                        xp[:, k * TC + tg * 512:k * TC + tg * 512 + 512],
                                        start=(k == 0), stop=(k == NCH - 1)), waits=w)
                mmu = None
                for k in range(NCH):
                    w = [tku, hstt_done[bi]]
                    mmu = p.do("tensor", lambda e, k=k, psB=psB, bi=bi, tg=tg:
                               e.matmul(psB, wu[bi][:, k * 128:(k + 1) * 128],
                                        xp[:, k * TC + tg * 512:k * TC + tg * 512 + 512],
                                        start=(k == 0), stop=(k == NCH - 1)), waits=w)
                gu_cons[bi] = mmu
                if quant:
                    sv = p.do("scalar", lambda e, psA=psA, bi=bi, ic=ic:
                              e.activation(sg[bi][:, :], psA, AF.Silu,
                                           scale=gs_col[:, ic:ic + 1]),
                              waits=[mmg, hstt_done[bi]])
                else:
                    sv = p.do("scalar", lambda e, psA=psA, bi=bi:
                              e.activation(sg[bi][:, :], psA, AF.Silu),
                              waits=[mmg, hstt_done[bi]])
                silu_done[bi] = sv
                hseg = slice(j * 512, (j + 1) * 512)
                if quant:
                    hv = p.do("vector", lambda e, psB=psB, bi=bi, ic=ic, hseg=hseg,
                              hbuf=hbuf:
                              e.scalar_tensor_tensor(hbuf[:, hseg], psB,
                                                     us_col[:, ic:ic + 1], sg[bi][:, :],
                                                     ALU.mult, ALU.mult),
                              waits=[mmu, sv, h_used[ib % 2]])
                else:
                    hv = p.do("vector", lambda e, psB=psB, bi=bi, hseg=hseg, hbuf=hbuf:
                              e.tensor_mul(hbuf[:, hseg], psB, sg[bi][:, :]),
                              waits=[mmu, sv, h_used[ib % 2]])
                hstt_done[bi] = hv
                if dbg is not None and tg == 0 and ib == 0 and j == 0:
                    p.do("gpsimd", lambda e: e.dma_start(
                        out=dbg["wg0"][:, :], in_=wu[0][:, 0:2048]),
                        waits=[mmg], sem="sG", by=16)
                    duma = p.do("vector", lambda e, psA=psA: e.tensor_copy(
                        dbg_sb2[:, :], psA), waits=[mmg])
                    p.do("gpsimd", lambda e: e.dma_start(
                        out=dbg["psa"][:, :], in_=dbg_sb2[:, :]), waits=[duma], sem="sG", by=16)
                    p.do("gpsimd", lambda e: e.dma_start(
                        out=dbg["sg"][:, :], in_=sg[0][:, :]), waits=[sv], sem="sG", by=16)
                    dum = p.do("vector", lambda e, psB=psB: e.tensor_copy(
                        dbg_sb[:, :], psB), waits=[mmu])
                    p.do("gpsimd", lambda e: e.dma_start(
                        out=dbg["psb"][:, :], in_=dbg_sb[:, :]), waits=[dum], sem="sG", by=16)
                    p.full_barrier()
                if dbg is not None and tg == 0 and ib == 0:
                    p.do("gpsimd", lambda e, j=j, hbuf=hbuf, hseg=hseg: e.dma_start(
                        out=dbg["h"][j * 128:(j + 1) * 128, :],
                        in_=hbuf[:, hseg]), waits=[hv], sem="sG", by=16)
                    p.full_barrier()
                if quant:
                    hsum_tok = p.do("tensor", lambda e, hseg=hseg, ic=ic, hbuf=hbuf:
                                    e.matmul(ps_hsum[0:1, :], c.ones_col_bf[:, 0:1],
                                             hbuf[:, hseg], start=(ic == 0),
                                             stop=(ic == NIC - 1)),
                                    waits=[hv, c.consts, hsum_prev_evac])
                else:
                    psH = ps_h[:, (j % 2) * 512:(j % 2 + 1) * 512]
                    mmh = p.do("tensor", lambda e, psH=psH, hseg=hseg, hbuf=hbuf:
                               e.matmul(psH, h128_sb[:, :], hbuf[:, hseg],
                                        start=True, stop=True),
                               waits=[hv, h2_done[j % 2]])
                    h2v = p.do("scalar", lambda e, psH=psH, hseg=hseg, h2buf=h2buf:
                               e.activation(h2buf[:, hseg], psH, AF.Copy),
                               waits=[mmh, h_used[ib % 2]])
                    h2_done[j % 2] = h2v
                    hstt_done[bi] = hv
            rhsbuf = hbuf if quant else h2buf
            rhs_ready = hstt_done[0] if quant else None
            for oc in range(NCH):
                di = (ib * NCH + oc) % 2
                tkd = p.do(dwe, lambda e, ib=ib, oc=oc, di=di: e.dma_start(
                    out=wd[di][:, :],
                    in_=dwT[oc * 128:(oc + 1) * 128, ib * 1024:(ib + 1) * 1024]),
                    waits=[d_cons[di]], sem=dws, by=16)
                psC = ps_c[:, (oc % 2) * 512:(oc % 2 + 1) * 512]
                mmd = None
                for k in range(IBC):
                    w = [tkd, psC_guard[oc % 2]]
                    if k == 0:
                        w.append(hstt_done[0])
                        w.append(hstt_done[1])
                        if not quant:
                            w.append(h2_done[0])
                            w.append(h2_done[1])
                    mmd = p.do("tensor", lambda e, k=k, psC=psC, di=di, rhsbuf=rhsbuf:
                               e.matmul(psC, wd[di][:, k * 128:(k + 1) * 128],
                                        rhsbuf[:, k * 512:(k + 1) * 512],
                                        start=(k == 0), stop=(k == IBC - 1)),
                               waits=w)
                d_cons[di] = mmd
                h_used[ib % 2] = mmd
                aseg = slice(oc * 512, (oc + 1) * 512)
                if ib == 0:
                    av = p.do("vector", lambda e, psC=psC, aseg=aseg:
                              e.tensor_copy(acc[:, aseg], psC),
                              waits=[mmd, acc_done[oc]])
                    acc_done[oc] = av
                    psC_guard[oc % 2] = av
                else:
                    av = p.do("vector", lambda e, psC=psC, aseg=aseg:
                              e.tensor_add(acc[:, aseg], acc[:, aseg], psC),
                              waits=[mmd, acc_done[oc]])
                acc_done[oc] = av
                psC_guard[oc % 2] = av
        if quant:
            ve = p.do("vector", lambda e: e.tensor_copy(hsum_sb[:, :], ps_hsum[0:1, :]),
                      waits=[hsum_tok])
            hsum_prev_evac = ve
            if dbg is not None and tg == 0:
                p.do("gpsimd", lambda e: e.dma_start(
                    out=dbg["hsum"][0:1, :], in_=hsum_sb[0:1, :]),
                    waits=[ve], sem="sG", by=16)
                p.do("gpsimd", lambda e: e.dma_start(
                    out=dbg["acc"][:, :].rearrange("(k p) t -> p k t", p=128),
                    in_=acc[:, :].rearrange("p (k t) -> p k t", k=NCH)),
                    waits=[acc_done[NCH - 1]], sem="sG", by=16)
                p.full_barrier()
            bc = p.do("tensor", lambda e: e.matmul(ps_d[:, :], c.ones_row_f[0:1, :],
                                                   hsum_sb[0:1, :], start=True,
                                                   stop=True),
                      waits=[ve, c.consts, psd_prev_use])
            t2_done = [None, None]
            for oc in range(NCH):
                aseg = slice(oc * 512, (oc + 1) * 512)
                xseg = slice(oc * TC + tg * 512, oc * TC + tg * 512 + 512)
                ti = oc % 2
                v1 = p.do("vector", lambda e, aseg=aseg, ti=ti:
                          e.scalar_tensor_tensor(t2[:, ti * 512:(ti + 1) * 512],
                                                 ps_d[:, :], -128.0, acc[:, aseg],
                                                 ALU.mult, ALU.add),
                          waits=[bc, acc_done[oc], t2_done[ti]])
                v2 = p.do("vector", lambda e, xseg=xseg, oc=oc, ti=ti:
                          e.scalar_tensor_tensor(c.x[:, xseg],
                                                 t2[:, ti * 512:(ti + 1) * 512],
                                                 dshalf_col[:, oc:oc + 1], c.x[:, xseg],
                                                 ALU.mult, ALU.add),
                          waits=[v1])
                t2_done[ti] = v2
                psd_prev_use = v1
                acc_done[oc] = v1
                final_toks.append(v2)
        else:
            for oc in range(NCH):
                aseg = slice(oc * 512, (oc + 1) * 512)
                xseg = slice(oc * TC + tg * 512, oc * TC + tg * 512 + 512)
                v2 = p.do("vector", lambda e, aseg=aseg, xseg=xseg:
                          e.scalar_tensor_tensor(c.x[:, xseg], acc[:, aseg], 0.5,
                                                 c.x[:, xseg], ALU.mult, ALU.add),
                          waits=[acc_done[oc]])
                acc_done[oc] = v2
                final_toks.append(v2)
    ph.close()
    return final_toks


def raw_emit(p, eng, f, waits=()):
    wl = []
    for tok in waits:
        if tok is None:
            continue
        sn, v = tok
        key = (eng, sn)
        if p._lastwait.get(key, -1) >= v:
            continue
        p._lastwait[key] = v
        wl.append((sn, v))
    sems = p.sems

    def emit(engine, f=f, wl=tuple(wl)):
        for (sn, v) in wl:
            engine.wait_ge(sems[sn], v)
        f(engine)
    p.q[eng].append(emit)


def rowop(p, eng, fn, waits):
    return p.do(eng, fn, waits=waits)


def emit_l0attnA(p, es, c, xp, xp_deps, W, dr, q_int):
    nc = p.nc
    NH = 2
    ph = ExitStack()
    stage = ph.enter_context(nc.sbuf_tensor("a0_stage", [128, 2 * 512], BF16, side="right"))
    wbuf = [ph.enter_context(nc.sbuf_tensor(f"a0_w{i}", [128, NCH * 128], BF16))
            for i in range(2)]
    ps_p = ph.enter_context(nc.psum_tensor("a0_psp", [128, 4 * 512], F32, side="right"))
    store_toks = {"k": [], "v": []}
    evac_guard = [None] * 4
    slab_cons = [None, None]
    qint_done = []

    def projection(which, wdram, bias_col):
        nonlocal slab_cons
        for oc in range(NCH):
            bi = oc % 2
            tok = p.do("gpsimd", lambda e, oc=oc, bi=bi, wdram=wdram: e.dma_start(
                out=wbuf[bi][:, :], in_=wdram[oc * 128:(oc + 1) * 128, :]),
                waits=[slab_cons[bi]], sem="sG", by=16)
            for tg in range(2):
                sl = (bi * 2 + tg)
                pslc = ps_p[:, sl * 512:(sl + 1) * 512]
                mm = None
                for k in range(NCH):
                    w = [tok, evac_guard[sl]]
                    if k == 0:
                        w = w + xp_deps
                    mm = p.do("tensor", lambda e, k=k, pslc=pslc, bi=bi, tg=tg:
                              e.matmul(pslc, wbuf[bi][:, k * 128:(k + 1) * 128],
                                       xp[:, k * TC + tg * 512:k * TC + tg * 512 + 512],
                                       start=(k == 0), stop=(k == NCH - 1)), waits=w)
                if which == "q":
                    qv = q_int.rearrange("p (h tt cc) -> p h cc tt", h=NH, cc=16)
                    ev = p.do("scalar", lambda e, pslc=pslc, tg=tg, oc=oc, qv=qv:
                              e.activation(qv[:, tg:tg + 1, oc:oc + 1, :], pslc,
                                           AF.Identity, bias=bias_col[:, oc:oc + 1]),
                              waits=[mm])
                    qint_done.append(ev)
                else:
                    sseg = slice(tg * 512, (tg + 1) * 512)
                    ev = p.do("scalar", lambda e, pslc=pslc, sseg=sseg, oc=oc:
                              e.activation(stage[:, sseg], pslc, AF.Identity,
                                           bias=bias_col[:, oc:oc + 1]),
                              waits=[mm] + ([store_toks[which][-2]]
                                            if len(store_toks[which]) >= 2 else []))
                    dst = dr["k_in"] if which == "k" else dr["v_in"]
                    st = p.do("sync", lambda e, dst=dst, oc=oc, tg=tg, sseg=sseg:
                              e.dma_start(out=dst[oc * 128:(oc + 1) * 128,
                                                  tg * 512:(tg + 1) * 512],
                                          in_=stage[:, sseg]),
                              waits=[ev], sem="sY", by=16)
                    store_toks[which].append(st)
                evac_guard[sl] = ev
            slab_cons[bi] = mm

    projection("k", W["Wk"], W["bk_col"])
    projection("v", W["Wv"], W["bv_col"])
    projection("q", W["Wq"], W["bq_col"])
    qd = p.do("gpsimd", lambda e: e.dma_start(out=dr["q_d"][:, :], in_=q_int[:, :]),
              waits=qint_done, sem="sG", by=16)
    ph.close()
    return [qd] + store_toks["k"][-1:] + store_toks["v"][-1:]


def emit_l0attnB(p, es, c, W, dr, mask4_sb, mask_dep):
    nc = p.nc
    NH = 2
    SCALE = float(1.0 / np.sqrt(128.0))
    K_sb = es.enter_context(nc.sbuf_tensor("b0_K", [128, NH * T], BF16))
    v_tok = es.enter_context(nc.sbuf_tensor("b0_V", [128, NH * T], BF16))
    q_int = es.enter_context(nc.sbuf_tensor("b0_q", [128, NH * T], BF16, side="right"))
    kld = p.do("sync", lambda e: e.dma_start(out=K_sb[:, :], in_=dr["K_in"][:, :]),
               sem="sY", by=16)
    vld = p.do("sync", lambda e: e.dma_start(out=v_tok[:, :], in_=dr["V_in"][:, :]),
               sem="sY", by=16)
    qld = p.do("sync", lambda e: e.dma_start(out=q_int[:, :], in_=dr["Q_in"][:, :]),
               sem="sY", by=16)
    qdone = [qld]
    vld_last = vld

    negm_cols = es.enter_context(nc.sbuf_tensor("a0_negm", [128, 32], F32, side="right"))
    csx = ExitStack()
    qsq = csx.enter_context(nc.sbuf_tensor("a0_qsq", [128, 512], BF16, side="right"))
    rows = csx.enter_context(nc.sbuf_tensor("a0_csrows", [1, 640], F32, side="right"))
    ps_cs = csx.enter_context(nc.psum_tensor("a0_pscs", [1, 1024], F32, side="right"))
    ps_nm = csx.enter_context(nc.psum_tensor("a0_psnm", [128, 32], F32, side="right"))

    def rm(j, w=1):
        return rows[0:1, j:j + w]
    sq_tok = None
    nm_last = None
    for h in range(NH):
        red = sq_tok
        for g in range(16):
            sl = (g % 2) * 512
            vq = p.do("vector", lambda e, h=h, g=g: e.tensor_mul(
                qsq[:, :], q_int[:, h * T + g * 512:h * T + (g + 1) * 512],
                q_int[:, h * T + g * 512:h * T + (g + 1) * 512]),
                waits=qdone + [red])
            mm = p.do("tensor", lambda e, g=g, sl=sl: e.matmul(
                ps_cs[0:1, sl:sl + 512], c.ones_col_bf[:, 0:1],
                qsq[:, :], start=True, stop=True),
                waits=[vq, red, c.consts])
            rc = p.do("vector", lambda e, sl=sl: e.tensor_copy(
                rm(96, 512), ps_cs[0:1, sl:sl + 512]), waits=[mm, red])
            red = p.do("vector", lambda e, h=h, g=g: e.reduce_max(
                rm(h * 16 + g), rm(96, 512), axis=AXX), waits=[rc])
        red2 = red
        for g in range(16):
            sl = (g % 2) * 512
            vk = p.do("vector", lambda e, h=h, g=g: e.tensor_mul(
                qsq[:, :], K_sb[:, h * T + g * 512:h * T + (g + 1) * 512],
                K_sb[:, h * T + g * 512:h * T + (g + 1) * 512]),
                waits=[red2, kld])
            mm = p.do("tensor", lambda e, g=g, sl=sl: e.matmul(
                ps_cs[0:1, sl:sl + 512], c.ones_col_bf[:, 0:1],
                qsq[:, :], start=True, stop=True),
                waits=[vk, red2])
            rc2 = p.do("vector", lambda e, sl=sl: e.tensor_copy(
                rm(96, 512), ps_cs[0:1, sl:sl + 512]), waits=[mm, red2])
            red2 = p.do("vector", lambda e, g=g: e.reduce_max(
                rm(32 + g), rm(96, 512), axis=AXX), waits=[rc2])
        sq_tok = red2
        km = p.do("vector", lambda e, h=h: e.reduce_max(rm(48 + h), rm(32, 16),
                                                        axis=AXX), waits=[red2])
        for g in range(16):
            idx = h * 16 + g
            t2 = p.do("vector", lambda e, h=h, idx=idx: e.tensor_single_scalar(
                rm(50), rm(idx), rm(48 + h), ALU.mult), waits=[km, nm_last])
            t3 = p.do("vector", lambda e: e.tensor_scalar_mul(
                rm(50), rm(50), 1.0 / 128.0), waits=[t2])
            t4 = p.do("scalar", lambda e: e.activation(rm(50), rm(50), AF.Sqrt),
                      waits=[t3])
            nm_last = p.do("vector", lambda e, idx=idx: e.tensor_scalar_mul(
                rm(64 + idx), rm(50), -1.0), waits=[t4])
    bc_mms = None
    for idx in range(32):
        bc_mms = p.do("tensor", lambda e, idx=idx: e.matmul(
            ps_nm[:, idx:idx + 1], c.ones_row_f[0:1, :], rm(64 + idx),
            start=True, stop=True), waits=[nm_last, c.consts])
    nm_ev = p.do("vector", lambda e: e.tensor_copy(negm_cols[:, :], ps_nm[:, :]),
                 waits=[bc_mms])
    p.full_barrier()
    csx.close()

    ax = ExitStack()
    wo_sb = es.enter_context(nc.sbuf_tensor("b0_wo", [128, NH * D], BF16))
    O_sb = es.enter_context(nc.sbuf_tensor("b0_Og", [128, 2 * NH * 512], BF16))
    pt = [ax.enter_context(nc.sbuf_tensor(f"a0_pt{i}", [128, 512], BF16, side="right"))
          for i in range(3)]
    ps_s = ax.enter_context(nc.psum_tensor("a0_pss", [128, 3 * 512], F32, side="right"))
    ps_o = ax.enter_context(nc.psum_tensor("a0_pso", [128, 512], F32, side="right"))
    ps_l = ax.enter_context(nc.psum_tensor("a0_psl", [1, 512], F32, side="right"))
    ps_rb = ax.enter_context(nc.psum_tensor("a0_psrb", [128, 512], F32, side="right"))
    ps_w = ax.enter_context(nc.psum_tensor("b0_psw", [128, 2 * 512], F32, side="right"))
    rl_sb = ax.enter_context(nc.sbuf_tensor("a0_rl", [1, 512], F32, side="right"))
    rb_sb = ax.enter_context(nc.sbuf_tensor("a0_rb", [128, 512], F32, side="right"))
    rst = ax.enter_context(nc.sbuf_tensor("b0_rst", [128, 2 * 512], BF16, side="right"))
    wol = p.do("gpsimd", lambda e: e.dma_start(
        out=wo_sb[:, :].rearrange("p (h o) -> p h o", h=NH),
        in_=W["wo_my"][:, :].rearrange("(h p) o -> p h o", p=128)),
        sem="sG", by=16)
    exp_guard = [None, None, None]
    av_guard = [None, None, None]
    o_ev_prev = None
    rb_prev = None
    rl_prev = None
    wo_ev_guard = [None, None]
    wo_st_guard = [None, None]
    O_guard = [None, None]
    st_last = None
    for g in range(16):
        ncs = 4 * g + 4
        o_toks = []
        for h in range(NH):
            l_last = None
            o_last = None
            for cc in range(ncs):
                sl = cc % 3
                pslc = ps_s[:, sl * 512:(sl + 1) * 512]
                mm = p.do("tensor", lambda e, h=h, g=g, cc=cc, pslc=pslc:
                          e.matmul(pslc, K_sb[:, h * T + cc * 128:h * T + (cc + 1) * 128],
                                   q_int[:, h * T + g * 512:h * T + (g + 1) * 512],
                                   start=True, stop=True),
                          waits=[exp_guard[sl], kld, nm_ev])
                md = None
                if cc >= 4 * g:
                    d = cc - 4 * g
                    md = p.do("vector", lambda e, pslc=pslc, d=d: e.tensor_add(
                        pslc, pslc, mask4_sb[:, d * 512:(d + 1) * 512]),
                        waits=[mm, mask_dep])
                ex = p.do("scalar", lambda e, pslc=pslc, sl=sl, h=h, g=g:
                          e.activation(pt[sl][:, :], pslc, AF.Exp,
                                       bias=negm_cols[:, h * 16 + g:h * 16 + g + 1],
                                       scale=SCALE),
                          waits=[mm, md, av_guard[sl]])
                exp_guard[sl] = ex
                lm = p.do("tensor", lambda e, sl=sl, cc=cc, ncs=ncs: e.matmul(
                    ps_l[0:1, :], c.ones_col_bf[:, 0:1], pt[sl][:, :],
                    start=(cc == 0), stop=(cc == ncs - 1)),
                    waits=[ex, rl_prev])
                om = p.do("tensor", lambda e, sl=sl, cc=cc, ncs=ncs, h=h: e.matmul(
                    ps_o[:, :], v_tok[:, (h * 64 + cc) * 128:(h * 64 + cc + 1) * 128],
                    pt[sl][:, :], start=(cc == 0), stop=(cc == ncs - 1)),
                    waits=[ex, o_ev_prev, vld])
                av_guard[sl] = om
                l_last, o_last = lm, om
            rl = p.do("vector", lambda e: e.reciprocal(rl_sb[0:1, :], ps_l[0:1, :]),
                      waits=[l_last, rb_prev])
            rl_prev = rl
            bcm = p.do("tensor", lambda e: e.matmul(
                ps_rb[:, :], c.ones_row_f[0:1, :], rl_sb[0:1, :],
                start=True, stop=True), waits=[rl])
            rbe = p.do("vector", lambda e: e.tensor_copy(rb_sb[:, :], ps_rb[:, :]),
                       waits=[bcm])
            rb_prev = rbe
            gb = g % 2
            oe = p.do("vector", lambda e, h=h, gb=gb: e.tensor_mul(
                O_sb[:, (gb * NH + h) * 512:(gb * NH + h + 1) * 512], ps_o[:, :],
                rb_sb[:, :]), waits=[rbe, o_last, O_guard[gb]])
            o_ev_prev = oe
            o_toks.append(oe)
        gb = g % 2
        for ti in range(4):
            tc = 4 * g + ti
            for og in range(4):
                sl = og % 2
                pslc = ps_w[:, sl * 512:(sl + 1) * 512]
                mm = None
                for h in range(NH):
                    mm = p.do("tensor", lambda e, ti=ti, og=og, h=h, gb=gb, pslc=pslc:
                              e.matmul(pslc,
                                       O_sb[:, (gb * NH + h) * 512 + ti * 128:
                                            (gb * NH + h) * 512 + (ti + 1) * 128],
                                       wo_sb[:, h * D + og * 512:h * D + (og + 1) * 512],
                                       start=(h == 0), stop=(h == NH - 1)),
                              waits=[wol, o_toks[-1], wo_ev_guard[sl]])
                O_guard[gb] = mm
                ev = p.do("vector", lambda e, pslc=pslc, sl=sl: e.tensor_copy(
                    rst[:, sl * 512:(sl + 1) * 512], pslc),
                    waits=[mm, wo_st_guard[sl]])
                wo_ev_guard[sl] = ev
                st_last = p.do("sync", lambda e, tc=tc, og=og, sl=sl: e.dma_start(
                    out=dr["rs_in"][tc * 128:(tc + 1) * 128, og * 512:(og + 1) * 512],
                    in_=rst[:, sl * 512:(sl + 1) * 512]), waits=[ev], sem="sY", by=16)
                wo_st_guard[sl] = st_last
    ccrs = p.do("gpsimd", lambda e: e.collective_compute(
        "ReduceScatter", ALU.add, [list(range(NCORE))],
        [dr["rs_in"][:, :].opt()], [dr["rs_out"][:, :].opt()]),
        waits=[st_last], sem="sC", by=1)
    p.full_barrier()
    ax.close()

    fx = ExitStack()
    tin = [fx.enter_context(nc.sbuf_tensor(f"b0_tin{i}", [128, D], BF16))
           for i in range(2)]
    ps_t = fx.enter_context(nc.psum_tensor("b0_pst", [128, 2 * 1024], BF16, side="right"))
    idb = fx.enter_context(nc.sbuf_tensor("b0_idb", [128, 128], BF16))
    idl = p.do("sync", lambda e: e.dma_start(out=idb[:, :], in_=W["ident_bf"]),
               sem="sY", by=16)
    res_toks = []
    tld = [None, None]
    tpe = [None, None]
    evac = [None, None]
    for tc in range(8):
        bi = tc % 2
        tld[bi] = p.do("sync", lambda e, tc=tc, bi=bi: e.dma_start(
            out=tin[bi][:, :], in_=dr["rs_out"][tc * 128:(tc + 1) * 128, :]),
            waits=[ccrs, tpe[bi]], sem="sY", by=16)
        for kc in range(NCH):
            sl = kc % 2
            mm = p.do("tensor", lambda e, kc=kc, bi=bi, sl=sl: e.transpose(
                ps_t[:, sl * 1024:sl * 1024 + 128], tin[bi][:, kc * 128:(kc + 1) * 128],
                idb[:, :]), waits=[tld[bi], idl, evac[sl]])
            if kc == NCH - 1:
                tpe[bi] = mm
            xseg = slice(kc * TC + tc * 128, kc * TC + (tc + 1) * 128)
            ev = p.do("vector", lambda e, xseg=xseg, sl=sl, kc=kc:
                      e.scalar_tensor_tensor(c.x[:, xseg],
                                             ps_t[:, sl * 1024:sl * 1024 + 128],
                                             W["bo_col"][:, kc:kc + 1], c.x[:, xseg],
                                             ALU.add, ALU.add), waits=[mm])
            evac[sl] = ev
            res_toks.append(ev)
    fx.close()
    return res_toks


def emit_l1attnA(p, es, c, x_dep, W, dr):
    nc = p.nc
    ph = ExitStack()
    h_sb = ph.enter_context(nc.sbuf_tensor("a1_h", [128, NCH * TC], BF16, side="right"))
    rows, h_deps = emit_ln_prescale(p, es, c, None, x_dep, "a1", l1_combined=True,
                                    gamma_col=W["gamma_col"], beta_col=W["beta_col"],
                                    h_out=h_sb)
    rd = p.do("sync", lambda e: e.dma_start(out=dr["rows_d"][0:1, 0:TC],
                                            in_=rows[0:1, 0:TC]),
              waits=h_deps, sem="sY", by=16)
    rd2 = p.do("sync", lambda e: e.dma_start(out=dr["rows_d"][0:1, TC:2 * TC],
                                             in_=rows[0:1, 5 * TC:6 * TC]),
               waits=h_deps, sem="sY", by=16)
    ex = ExitStack()
    wbuf = [ex.enter_context(nc.sbuf_tensor(f"a1_w{i}", [128, NCH * 128], BF16))
            for i in range(2)]
    ps_p = ex.enter_context(nc.psum_tensor("a1_psp", [128, 4 * 512], F32, side="right"))
    q2_sb = ex.enter_context(nc.sbuf_tensor("a1_q2", [128, 4 * TC], BF16, side="right"))
    gtmp = ex.enter_context(nc.sbuf_tensor("a1_gt", [128, 2 * TC], F32, side="right"))
    geg = ex.enter_context(nc.sbuf_tensor("a1_geg", [128, 2 * TC], BF16, side="right"))
    oc_order = [0, 1, 2, 3]
    for j in range(32):
        oc_order += [36 + j, 4 + j]
    slab_cons = [None, None]
    evg = [None] * 4
    stq = []
    gelu_tok = {}
    geg_prev = {}
    q2_toks = []
    for oi, oc in enumerate(oc_order):
        bi = oi % 2
        tok = p.do("gpsimd", lambda e, oc=oc, bi=bi: e.dma_start(
            out=wbuf[bi][:, :], in_=W["expandT"][oc * 128:(oc + 1) * 128, :]),
            waits=[slab_cons[bi]], sem="sG", by=16)
        for tg in range(2):
            sl = bi * 2 + tg
            pslc = ps_p[:, sl * 512:(sl + 1) * 512]
            mm = None
            for k in range(NCH):
                w = [tok, evg[sl]]
                if k == 0:
                    w = w + h_deps
                mm = p.do("tensor", lambda e, k=k, pslc=pslc, bi=bi, tg=tg:
                          e.matmul(pslc, wbuf[bi][:, k * 128:(k + 1) * 128],
                                   h_sb[:, k * TC + tg * 512:k * TC + tg * 512 + 512],
                                   start=(k == 0), stop=(k == NCH - 1)), waits=w)
            seg5 = slice(tg * 512, (tg + 1) * 512)
            if oc < 2:
                ev = p.do("scalar", lambda e, pslc=pslc, oc=oc, tg=tg:
                          e.activation(q2_sb[:, oc * TC + tg * 512:oc * TC + tg * 512 + 512],
                                       pslc, AF.Copy), waits=[mm])
                q2_toks.append(ev)
            elif oc < 4:
                ev = p.do("scalar", lambda e, pslc=pslc, oc=oc, tg=tg:
                          e.activation(q2_sb[:, oc * TC + tg * 512:
                                             oc * TC + tg * 512 + 512],
                                       pslc, AF.Copy), waits=[mm])
                st = p.do("sync", lambda e, oc=oc, tg=tg: e.dma_start(
                    out=dr["k2_in"][(oc - 2) * 128:(oc - 1) * 128,
                                    tg * 512:(tg + 1) * 512],
                    in_=q2_sb[:, oc * TC + tg * 512:oc * TC + tg * 512 + 512]),
                    waits=[ev], sem="sY", by=16)
                stq.append(st)
                ev = st
            elif oc >= 36:
                jj = oc - 36
                po = jj % 2
                ev = p.do("scalar", lambda e, pslc=pslc, po=po, seg5=seg5:
                          e.activation(gtmp[:, po * TC:po * TC + TC][:, seg5], pslc,
                                       AF.Gelu), waits=[mm, geg_prev.get((po, tg))])
                gelu_tok[(jj, tg)] = ev
            else:
                jj = oc - 4
                po = jj % 2
                ev = p.do("vector", lambda e, pslc=pslc, po=po, seg5=seg5:
                          e.tensor_mul(geg[:, po * TC:po * TC + TC][:, seg5], pslc,
                                       gtmp[:, po * TC:po * TC + TC][:, seg5]),
                          waits=[mm, gelu_tok[(jj, tg)]])
                dst = dr["gl"] if jj < 16 else dr["val_in"]
                r0 = (jj % 16) * 128
                st = p.do("gpsimd", lambda e, dst=dst, r0=r0, tg=tg, po=po, seg5=seg5:
                          e.dma_start(out=dst[r0:r0 + 128, tg * 512:(tg + 1) * 512],
                                      in_=geg[:, po * TC:po * TC + TC][:, seg5]),
                          waits=[ev], sem="sG", by=16)
                geg_prev[(po, tg)] = st
                ev = st
            evg[sl] = ev
        slab_cons[bi] = mm
    qd = p.do("gpsimd", lambda e: e.dma_start(out=dr["q2_d"][:, :],
                                              in_=q2_sb[:, 0:2 * TC]),
               waits=q2_toks, sem="sG", by=16)
    p.full_barrier()
    ex.close()
    ph.close()
    return [qd] + stq[-1:]


def emit_l1attnB(p, es, c, x_dep, W, dr):
    nc = p.nc
    SC2 = float(1.0 / 16.0)
    NSB = 8
    mrow = es.enter_context(nc.sbuf_tensor("a1_mrow", [1, 2048], F32, side="right"))
    sp_col = es.enter_context(nc.sbuf_tensor("a1_spcol", [128, 1], F32, side="right"))
    negm2 = es.enter_context(nc.sbuf_tensor("a1_negm2", [128, 2], F32, side="right"))
    biasc = es.enter_context(nc.sbuf_tensor("a1_biasc", [128, 2 * 64], F32, side="right"))
    q2_sb = es.enter_context(nc.sbuf_tensor("a1_q2", [128, 2 * TC], BF16, side="right"))
    q2l = p.do("sync", lambda e: e.dma_start(out=q2_sb[:, :], in_=dr["q2_d"][:, :]),
               sem="sY", by=16)

    def mr(j, w=1):
        return mrow[0:1, j:j + w]
    sx = ExitStack()
    ps_sm = sx.enter_context(nc.psum_tensor("a1_pssm", [128, 3 * 512], F32, side="right"))
    t = p.do("vector", lambda e: e.tensor_scalar_mul(
        mr(1), W["pos_sb"][0:1, 0:1], -1.0), waits=[x_dep])
    t = p.do("vector", lambda e: e.tensor_max(
        mr(1), mr(1), W["pos_sb"][0:1, 0:1]), waits=[t])
    t = p.do("scalar", lambda e: e.activation(mr(2), mr(1), AF.Exp, scale=-1.0),
             waits=[t])
    t = p.do("scalar", lambda e: e.activation(mr(2), mr(2), AF.Ln, bias=1.0),
             waits=[t])
    t2 = p.do("vector", lambda e: e.tensor_scalar_max(
        mr(1), W["pos_sb"][0:1, 0:1], 0.0), waits=[t])
    sp_tok = p.do("vector", lambda e: e.tensor_add(mr(0), mr(1), mr(2)), waits=[t2])
    t = p.do("vector", lambda e: e.tensor_single_scalar(
        mr(16, 512), W["iota512"][0:1, :], mr(0), ALU.mult), waits=[sp_tok])
    t2row_tok = p.do("vector", lambda e: e.tensor_scalar_mul(
        mr(16, 512), mr(16, 512), -16.0), waits=[t])
    bcm = p.do("tensor", lambda e: e.matmul(
        ps_sm[:, 0:1], c.ones_row_f[0:1, :], mr(0), start=True, stop=True),
        waits=[sp_tok, c.consts])
    spc_tok = p.do("vector", lambda e: e.tensor_copy(sp_col[:, :], ps_sm[:, 0:1]),
                   waits=[bcm])

    csx = ExitStack()
    sq2 = csx.enter_context(nc.sbuf_tensor("a1_sq2", [128, 2 * TC], BF16, side="right"))
    k2s0 = csx.enter_context(nc.sbuf_tensor("a1_k2cs", [128, 2 * TC], BF16))
    vq = p.do("vector", lambda e: e.tensor_mul(sq2[:, :], q2_sb[:, :], q2_sb[:, :]),
              waits=[q2l])
    red = None
    for g in range(2):
        mm = None
        for kk in range(2):
            mm = p.do("tensor", lambda e, g=g, kk=kk: e.matmul(
                ps_sm[0:1, 512:1024], c.ones_col_bf[:, 0:1],
                sq2[:, kk * TC + g * 512:kk * TC + g * 512 + 512],
                start=(kk == 0), stop=(kk == 1)), waits=[vq, red, c.consts])
        rc = p.do("vector", lambda e: e.tensor_copy(mr(544, 512), ps_sm[0:1, 512:1024]),
                  waits=[mm, red])
        red = p.do("vector", lambda e, g=g: e.reduce_max(
            mr(540 + g), mr(544, 512), axis=AXX), waits=[rc])
    kred = None
    for r in range(NCORE):
        kl = p.do("sync", lambda e, r=r: e.dma_start(
            out=k2s0[:, :].rearrange("p (d t) -> p d t", d=2),
            in_=dr["k2g"][256 * r:256 * (r + 1), :].rearrange("(d p) t -> p d t", p=128)),
            waits=[kred], sem="sY", by=16)
        vk = p.do("vector", lambda e: e.tensor_mul(sq2[:, :], k2s0[:, :], k2s0[:, :]),
                  waits=[kl, red])
        mm = None
        for tg in range(2):
            for kk in range(2):
                mm = p.do("tensor", lambda e, tg=tg, kk=kk: e.matmul(
                    ps_sm[0:1, 512 + tg * 512:1024 + tg * 512],
                    c.ones_col_bf[:, 0:1],
                    sq2[:, kk * TC + tg * 512:kk * TC + tg * 512 + 512],
                    start=(kk == 0), stop=(kk == 1)), waits=[vk, kred])
        rc = p.do("vector", lambda e: e.tensor_copy(mr(1024, 1024), ps_sm[0:1, 512:1536]),
                  waits=[mm, kred])
        kred = p.do("vector", lambda e, r=r: e.reduce_max(
            mr(528 + r), mr(1024, 1024), axis=AXX), waits=[rc])
    kmax = p.do("vector", lambda e: e.reduce_max(mr(536), mr(528, 8), axis=AXX),
                waits=[kred])
    nm_toks = []
    prev = kmax
    for g in range(2):
        t1 = p.do("vector", lambda e, g=g: e.tensor_single_scalar(
            mr(2), mr(540 + g), mr(536), ALU.mult), waits=[prev])
        t3 = p.do("vector", lambda e: e.tensor_scalar_mul(
            mr(2), mr(2), 1.0 / 256.0), waits=[t1])
        t4 = p.do("scalar", lambda e: e.activation(mr(2), mr(2), AF.Sqrt), waits=[t3])
        prev = p.do("vector", lambda e, g=g: e.tensor_scalar_mul(
            mr(540 + g), mr(2), -1.0), waits=[t4])
        nm_toks.append(prev)
    b1 = p.do("tensor", lambda e: e.matmul(ps_sm[:, 512:513], c.ones_row_f[0:1, :],
                                           mr(540), start=True, stop=True),
              waits=[nm_toks[-1], spc_tok])
    b2 = p.do("tensor", lambda e: e.matmul(ps_sm[:, 1024:1025], c.ones_row_f[0:1, :],
                                           mr(541), start=True, stop=True),
              waits=[b1])
    nme0 = p.do("vector", lambda e: e.tensor_copy(negm2[:, 0:1], ps_sm[:, 512:513]),
                waits=[b2])
    nme = p.do("vector", lambda e: e.tensor_copy(negm2[:, 1:2], ps_sm[:, 1024:1025]),
               waits=[nme0])
    bprev = None
    for g in range(2):
        for cc in range(64):
            bprev = p.do("vector", lambda e, g=g, cc=cc: e.scalar_tensor_tensor(
                biasc[:, g * 64 + cc:g * 64 + cc + 1],
                W["relpos"][:, g * 64 + cc:g * 64 + cc + 1], sp_col[:, 0:1],
                negm2[:, g:g + 1], ALU.mult, ALU.add),
                waits=[nme, spc_tok, bprev])
    bias_done = bprev
    p.full_barrier()
    csx.close()
    sx.close()

    ax = ExitStack()
    k2s = [ax.enter_context(nc.sbuf_tensor(f"a1_k2s{i}", [128, 2 * TC], BF16))
           for i in range(2)]
    v2buf = ax.enter_context(nc.sbuf_tensor("a1_v2", [128, NSB * D], BF16))
    pt_strip = ax.enter_context(nc.sbuf_tensor("a1_pt", [128, NSB * 512], BF16, side="right"))
    msk = ax.enter_context(nc.sbuf_tensor("a1_msk", [128, 2 * 512], F32, side="right"))
    acc2 = ax.enter_context(nc.sbuf_tensor("a1_acc2", [128, NCH * 512], F32, side="right"))
    rl_sb = ax.enter_context(nc.sbuf_tensor("a1_rl", [1, 512], F32, side="right"))
    rb_sb = ax.enter_context(nc.sbuf_tensor("a1_rb", [128, 512], F32, side="right"))
    ast = ax.enter_context(nc.sbuf_tensor("a1_ast", [128, 2 * 512], BF16, side="right"))
    ps_s = ax.enter_context(nc.psum_tensor("a1_pss", [128, 2 * 512], F32, side="right"))
    ps_l = ax.enter_context(nc.psum_tensor("a1_psl", [1, 512], F32, side="right"))
    ps_c = ax.enter_context(nc.psum_tensor("a1_psc", [128, 2 * 512], F32, side="right"))
    ps_rb = ax.enter_context(nc.psum_tensor("a1_psrb", [128, 512], F32, side="right"))
    exp_guard = [None, None]
    mload_guard = [None, None]
    k2_guard = [None, None]
    v2_guard = None
    acc_done = [None] * NCH
    psC_guard = [None, None]
    rl_prev = None
    rb_prev = None
    ast_prev = [None, None]
    attn_st = []
    for g in range(2):
        l_last = None
        for sb in range(NSB):
            kbi = sb % 2
            k2t = []
            for dc in range(2):
                k2t.append(p.do("sync", lambda e, sb=sb, dc=dc, kbi=kbi: e.dma_start(
                    out=k2s[kbi][:, dc * TC:(dc + 1) * TC],
                    in_=dr["k2g"][256 * sb + 128 * dc:256 * sb + 128 * (dc + 1), :]),
                    waits=[k2_guard[kbi]], sem="sY", by=16))
            v2t = []
            for j in range(NSB):
                v2t.append(p.do("sync", lambda e, sb=sb, j=j: e.dma_start_transpose(
                    v2buf[:, j * D:(j + 1) * D],
                    dr["valg"][2048 * sb:2048 * (sb + 1), j * 128:(j + 1) * 128]),
                    waits=[v2_guard], sem="sY", by=16))
            exp_toks = []
            for j in range(NSB):
                cc = sb * NSB + j
                sl = j % 2
                pslc = ps_s[:, sl * 512:(sl + 1) * 512]
                mt = p.do("gpsimd", lambda e, g=g, cc=cc, sl=sl: e.dma_start(
                    out=msk[:, sl * 512:(sl + 1) * 512],
                    in_=dr["maskseq"][(g * 64 + cc) * 128:(g * 64 + cc + 1) * 128, :]),
                    waits=[mload_guard[sl]], sem="sG", by=16)
                mm = None
                for dc in range(2):
                    mm = p.do("tensor", lambda e, dc=dc, g=g, j=j, kbi=kbi, pslc=pslc:
                              e.matmul(pslc, k2s[kbi][:, dc * TC + j * 128:
                                                      dc * TC + (j + 1) * 128],
                                       q2_sb[:, dc * TC + g * 512:dc * TC + g * 512 + 512],
                                       start=(dc == 0), stop=False),
                              waits=[k2t[dc], exp_guard[sl]] + ([q2l] if cc == 0 else []))
                mm = p.do("tensor", lambda e, pslc=pslc: e.matmul(
                    pslc, c.ones_row_f[0:1, :], mr(16, 512),
                    start=False, stop=True), waits=[t2row_tok])
                md = p.do("vector", lambda e, pslc=pslc, sl=sl: e.tensor_add(
                    pslc, pslc, msk[:, sl * 512:(sl + 1) * 512]), waits=[mm, mt])
                mload_guard[sl] = md
                exq = p.do("scalar", lambda e, pslc=pslc, j=j, g=g, cc=cc: e.activation(
                    pt_strip[:, j * 512:(j + 1) * 512], pslc, AF.Exp,
                    bias=biasc[:, g * 64 + cc:g * 64 + cc + 1], scale=SC2),
                    waits=[md, bias_done, v2_guard])
                exp_guard[sl] = exq
                exp_toks.append(exq)
                l_last = p.do("tensor", lambda e, j=j, cc=cc, g=g: e.matmul(
                    ps_l[0:1, :], c.ones_col_bf[:, 0:1],
                    pt_strip[:, j * 512:(j + 1) * 512],
                    start=(cc == 0), stop=(cc == 63)),
                    waits=[exq, rl_prev])
            k2_guard[kbi] = mm
            for vc in range(NCH):
                pslc = ps_c[:, (vc % 2) * 512:(vc % 2 + 1) * 512]
                mm = None
                for j in range(NSB):
                    mm = p.do("tensor", lambda e, j=j, vc=vc, pslc=pslc: e.matmul(
                        pslc, v2buf[:, j * D + vc * 128:j * D + (vc + 1) * 128],
                        pt_strip[:, j * 512:(j + 1) * 512],
                        start=(j == 0), stop=(j == NSB - 1)),
                        waits=[exp_toks[j], v2t[j], psC_guard[vc % 2]])
                aseg = slice(vc * 512, (vc + 1) * 512)
                if sb == 0:
                    av = p.do("vector", lambda e, pslc=pslc, aseg=aseg:
                              e.tensor_copy(acc2[:, aseg], pslc),
                              waits=[mm, acc_done[vc]])
                else:
                    av = p.do("vector", lambda e, pslc=pslc, aseg=aseg:
                              e.tensor_add(acc2[:, aseg], acc2[:, aseg], pslc),
                              waits=[mm, acc_done[vc]])
                acc_done[vc] = av
                psC_guard[vc % 2] = av
            v2_guard = mm
        rl = p.do("vector", lambda e: e.reciprocal(rl_sb[0:1, :], ps_l[0:1, :]),
                  waits=[l_last, rb_prev])
        rl_prev = rl
        bcm = p.do("tensor", lambda e: e.matmul(ps_rb[:, :], c.ones_row_f[0:1, :],
                                                rl_sb[0:1, :], start=True, stop=True),
                   waits=[rl])
        rbe = p.do("vector", lambda e: e.tensor_copy(rb_sb[:, :], ps_rb[:, :]),
                   waits=[bcm])
        rb_prev = rbe
        for vc in range(NCH):
            osl = vc % 2
            oe = p.do("vector", lambda e, vc=vc, osl=osl: e.tensor_mul(
                ast[:, osl * 512:(osl + 1) * 512], acc2[:, vc * 512:(vc + 1) * 512],
                rb_sb[:, :]), waits=[rbe, acc_done[vc], ast_prev[osl]])
            st = p.do("gpsimd", lambda e, vc=vc, g=g, osl=osl: e.dma_start(
                out=dr["attn"][vc * 128:(vc + 1) * 128, g * 512:(g + 1) * 512],
                in_=ast[:, osl * 512:(osl + 1) * 512]),
                waits=[oe], sem="sG", by=16)
            ast_prev[osl] = st
            attn_st.append(st)
        for vc in range(NCH):
            acc_done[vc] = attn_st[-1]
    p.full_barrier()
    ax.close()

    px0 = ExitStack()
    mbab = es.enter_context(nc.sbuf_tensor("a1_mbab", [128, 2 * TC], F32, side="right"))
    rows2 = px0.enter_context(nc.sbuf_tensor("a1_rows2", [1, 2 * TC], F32, side="right"))
    ps_bc2 = px0.enter_context(nc.psum_tensor("a1_psbc2", [128, 2 * TC], F32, side="right"))
    rld = p.do("sync", lambda e: e.dma_start(out=rows2[0:1, :], in_=dr["rows_d"][0:1, :]),
               sem="sY", by=16)
    bc = None
    for tg in range(2):
        p.do("tensor", lambda e, tg=tg: e.matmul(
            ps_bc2[:, tg * 512:(tg + 1) * 512], c.ones_row_f[0:1, :],
            rows2[0:1, tg * 512:(tg + 1) * 512], start=True, stop=True),
            waits=[c.consts, rld, attn_st[-1]])
        bc = p.do("tensor", lambda e, tg=tg: e.matmul(
            ps_bc2[:, TC + tg * 512:TC + (tg + 1) * 512], c.ones_row_f[0:1, :],
            rows2[0:1, TC + tg * 512:TC + (tg + 1) * 512],
            start=True, stop=True))
    mbe = p.do("vector", lambda e: e.tensor_copy(mbab[:, :], ps_bc2[:, :]), waits=[bc])
    p.full_barrier()
    px0.close()
    px = ExitStack()
    cat = px.enter_context(nc.sbuf_tensor("a1_cat", [128, 32 * TC], BF16, side="right"))
    wbuf3 = [px.enter_context(nc.sbuf_tensor(f"a1_w3{i}", [128, 32 * 128], BF16))
             for i in range(2)]
    ps_p3 = px.enter_context(nc.psum_tensor("a1_psp3", [128, 4 * 512], F32, side="right"))
    t3 = px.enter_context(nc.sbuf_tensor("a1_t3", [128, 2 * 512], F32, side="right"))
    cld = []
    for kc in range(32):
        src = dr["gl"] if kc < 16 else dr["attn"]
        r0 = (kc % 16) * 128
        cld.append(p.do("sync", lambda e, src=src, r0=r0, kc=kc: e.dma_start(
            out=cat[:, kc * TC:(kc + 1) * TC], in_=src[r0:r0 + 128, :]),
            waits=[attn_st[-1]], sem="sY", by=16))
    cat_dep = cld[-1]
    slab3 = [None, None]
    evg3 = [None] * 4
    t3_guard = [None, None]
    res_toks = []
    for oc in range(NCH):
        bi = oc % 2
        tok = p.do("gpsimd", lambda e, oc=oc, bi=bi: e.dma_start(
            out=wbuf3[bi][:, :], in_=W["projectT"][oc * 128:(oc + 1) * 128, :]),
            waits=[slab3[bi]], sem="sG", by=16)
        for tg in range(2):
            sl = bi * 2 + tg
            pslc = ps_p3[:, sl * 512:(sl + 1) * 512]
            mm = None
            for k in range(32):
                w = [tok, evg3[sl]]
                if k == 0:
                    w.append(cat_dep)
                mm = p.do("tensor", lambda e, k=k, pslc=pslc, bi=bi, tg=tg:
                          e.matmul(pslc, wbuf3[bi][:, k * 128:(k + 1) * 128],
                                   cat[:, k * TC + tg * 512:k * TC + tg * 512 + 512],
                                   start=(k == 0), stop=(k == 31)), waits=w)
            xseg = slice(oc * TC + tg * 512, oc * TC + tg * 512 + 512)
            tsl = slice((tg % 2) * 512, (tg % 2) * 512 + 512)
            v1 = p.do("vector", lambda e, xseg=xseg, tsl=tsl, tg=tg: e.tensor_sub(
                t3[:, tsl], c.x[:, xseg], mbab[:, tg * 512:(tg + 1) * 512]),
                waits=[mbe, t3_guard[tg % 2]])
            v2 = p.do("vector", lambda e, xseg=xseg, tsl=tsl, tg=tg: e.tensor_mul(
                t3[:, tsl], t3[:, tsl], mbab[:, TC + tg * 512:TC + (tg + 1) * 512]),
                waits=[v1])
            v3 = p.do("vector", lambda e, xseg=xseg, tsl=tsl: e.tensor_add(
                c.x[:, xseg], c.x[:, xseg], t3[:, tsl]), waits=[v2])
            v4 = p.do("vector", lambda e, xseg=xseg, pslc=pslc: e.tensor_add(
                c.x[:, xseg], c.x[:, xseg], pslc), waits=[v3, mm])
            evg3[sl] = v4
            t3_guard[tg % 2] = v2
            res_toks.append(v4)
        slab3[bi] = mm
    px.close()
    return res_toks


def _col_loads(p, nc, es, specs):
    out = {}
    tok = None
    for (name, ap, shape, dt) in specs:
        sb = es.enter_context(nc.sbuf_tensor("cl_" + name, shape, dt))
        tok = p.do("sync", lambda e, sb=sb, ap=ap: e.dma_start(out=sb[:, :], in_=ap),
                   sem="sY", by=16)
        out[name] = sb
    return out, tok


def build_prep():
    nc = bass.Bass("TRN2", target_bir_lowering=False)
    x_tok = nc.dram_tensor("x_tok", [TC, D], F32, kind="ExternalInput").ap()
    w2t = nc.dram_tensor("w2t", [IFF, D], F32, kind="ExternalInput").ap()
    h64 = nc.dram_tensor("h64", [64, 64], BF16, kind="ExternalInput").ap()
    ident = nc.dram_tensor("ident", [128, 128], F32, kind="ExternalInput").ap()
    x_fm = nc.dram_tensor("x_fm", [D, TC], F32, kind="ExternalOutput").ap()
    w2pp = nc.dram_tensor("w2pp", [IFF, D], BF16, kind="ExternalOutput").ap()
    p = P(nc)
    es = ExitStack()
    c = alloc_common(p, es)
    idf = es.enter_context(nc.sbuf_tensor("p_id", [128, 128], F32))
    h64s = es.enter_context(nc.sbuf_tensor("p_h64", [64, 64], BF16))
    xin = [es.enter_context(nc.sbuf_tensor(f"p_xin{i}", [128, D], F32))
           for i in range(2)]
    w2slab = [es.enter_context(nc.sbuf_tensor(f"p_w2s{i}", [64, D], BF16))
              for i in range(2)]
    wtmp = [es.enter_context(nc.sbuf_tensor(f"p_wt{i}", [64, D], BF16))
            for i in range(2)]
    ps_t = es.enter_context(nc.psum_tensor("p_pst", [128, 2 * 512], F32))
    ps_w = es.enter_context(nc.psum_tensor("p_psw", [64, 4 * 512], F32))
    tid = p.do("sync", lambda e: e.dma_start(out=idf[:, :], in_=ident), sem="sY", by=16)
    th = p.do("sync", lambda e: e.dma_start(out=h64s[:, :], in_=h64), sem="sY", by=16)
    xload = [None, None]
    tpe = [None, None]
    evac = [None, None]
    for tc in range(8):
        bi = tc % 2
        xload[bi] = p.do("sync", lambda e, tc=tc, bi=bi: e.dma_start(
            out=xin[bi][:, :], in_=x_tok[tc * 128:(tc + 1) * 128, :]),
            waits=[tpe[bi]], sem="sY", by=16)
        for kc in range(NCH):
            sl = kc % 2
            mm = p.do("tensor", lambda e, kc=kc, bi=bi, sl=sl: e.transpose(
                ps_t[:, sl * 512:sl * 512 + 128], xin[bi][:, kc * 128:(kc + 1) * 128],
                idf[:, :]), waits=[xload[bi], tid, evac[sl]])
            if kc == NCH - 1:
                tpe[bi] = mm
            evac[sl] = p.do("vector", lambda e, kc=kc, tc=tc, sl=sl: e.tensor_copy(
                c.x[:, kc * TC + tc * 128:kc * TC + (tc + 1) * 128],
                ps_t[:, sl * 512:sl * 512 + 128]), waits=[mm])
    stx = p.do("sync", lambda e: e.dma_start(out=fmview(x_fm), in_=xsbview(c.x)),
               waits=[evac[0], evac[1]], sem="sY", by=16)
    scale = float(1.0 / np.sqrt(IFF))
    wload = [None, None]
    wmm = [None, None]
    wev_last = None
    wev = [None, None]
    wst = [None, None]
    for pi in range(128):
        bi = pi % 2
        wload[bi] = p.do("gpsimd", lambda e, pi=pi, bi=bi: e.dma_start(
            out=w2slab[bi][:, :], in_=w2t[64 * pi:64 * (pi + 1), :]),
            waits=[wmm[bi]], sem="sG", by=16)
        mm = None
        for og in range(4):
            mm = p.do("tensor", lambda e, og=og, bi=bi: e.matmul(
                ps_w[:, og * 512:(og + 1) * 512], h64s[:, :],
                w2slab[bi][:, og * 512:(og + 1) * 512], start=True, stop=True),
                waits=[wload[bi], th, wev_last])
        wmm[bi] = mm
        ev = p.do("scalar", lambda e, bi=bi: e.activation(
            wtmp[bi][:, :], ps_w[:, :], AF.Copy, scale=scale),
            waits=[mm, wst[bi]])
        wev[bi] = ev
        wev_last = ev
        dst = w2pp.rearrange("(cp pp) o -> cp pp o", pp=128)[:, pi:pi + 1, :]
        dst = dst.rearrange("c q o -> c (q o)")
        wst[bi] = p.do("sync", lambda e, dst=dst, bi=bi: e.dma_start(
            out=dst, in_=wtmp[bi][:, :]), waits=[ev], sem="sY", by=16)
    p.wait_only("sync", [stx, wst[0], wst[1]] + p.dma_drain_toks())
    p.finalize()
    es.close()
    return nc


def build_finish():
    nc = bass.Bass("TRN2", target_bir_lowering=False)
    x_fm = nc.dram_tensor("x_fm", [D, TC], F32, kind="ExternalInput").ap()
    ident = nc.dram_tensor("ident", [128, 128], F32, kind="ExternalInput").ap()
    x_tok = nc.dram_tensor("x_tok", [TC, D], F32, kind="ExternalOutput").ap()
    p = P(nc)
    es = ExitStack()
    c = alloc_common(p, es)
    idf = es.enter_context(nc.sbuf_tensor("f_id", [128, 128], F32))
    xout = [es.enter_context(nc.sbuf_tensor(f"f_xo{i}", [128, D], F32))
            for i in range(2)]
    ps_t = es.enter_context(nc.psum_tensor("f_pst", [128, 2 * 512], F32))
    tid = p.do("sync", lambda e: e.dma_start(out=idf[:, :], in_=ident), sem="sY", by=16)
    load_x(p, c, x_fm)
    evac = [None, None]
    sts = [None, None]
    final = None
    for tc in range(8):
        bi = tc % 2
        for kc in range(NCH):
            sl = kc % 2
            mm = p.do("tensor", lambda e, kc=kc, tc=tc, sl=sl: e.transpose(
                ps_t[:, sl * 512:sl * 512 + 128],
                c.x[:, kc * TC + tc * 128:kc * TC + (tc + 1) * 128], idf[:, :]),
                waits=[c.x_tok, tid, evac[sl]])
            evac[sl] = p.do("vector", lambda e, kc=kc, bi=bi, sl=sl: e.tensor_copy(
                xout[bi][:, kc * 128:(kc + 1) * 128],
                ps_t[:, sl * 512:sl * 512 + 128]),
                waits=[mm, sts[bi]])
        final = p.do("sync", lambda e, tc=tc, bi=bi: e.dma_start(
            out=x_tok[tc * 128:(tc + 1) * 128, :], in_=xout[bi][:, :]),
            waits=[evac[0], evac[1]], sem="sY", by=16)
        sts[bi] = final
    p.wait_only("sync", [final] + p.dma_drain_toks())
    p.finalize()
    es.close()
    return nc


def build_l0ffn():
    nc = bass.Bass("TRN2", target_bir_lowering=False)
    x_fm = nc.dram_tensor("x_fm", [D, TC], F32, kind="ExternalInput").ap()
    gwT = nc.dram_tensor("gwT", [IFF, D], I32, kind="ExternalInput").ap()
    uwT = nc.dram_tensor("uwT", [IFF, D], I32, kind="ExternalInput").ap()
    dwT = nc.dram_tensor("dwT", [D, IFF], I32, kind="ExternalInput").ap()
    gs_in = nc.dram_tensor("gs_col", [128, 64], F32, kind="ExternalInput").ap()
    us_in = nc.dram_tensor("us_col", [128, 64], F32, kind="ExternalInput").ap()
    ds_in = nc.dram_tensor("ds_col", [128, 16], F32, kind="ExternalInput").ap()
    x_out = nc.dram_tensor("x_out", [D, TC], F32, kind="ExternalOutput").ap()
    p = P(nc)
    es = ExitStack()
    c = alloc_common(p, es)
    cols, ctok = _col_loads(p, nc, es, [
        ("gs", gs_in, [128, 64], F32), ("us", us_in, [128, 64], F32),
        ("ds", ds_in, [128, 16], F32)])
    dshalf = es.enter_context(nc.sbuf_tensor("l0f_dsh", [128, 16], F32))
    dtok = p.do("vector", lambda e: e.tensor_scalar_mul(dshalf[:, :],
                                                        cols["ds"][:, :], 0.5),
                waits=[ctok])
    load_x(p, c, x_fm)
    xp = es.enter_context(nc.sbuf_tensor("l0f_xp", [128, NCH * TC], BF16, side="right"))
    rows, pre = emit_ln_prescale(p, es, c, xp, c.x_tok, "f0")
    toks = emit_ffn(p, es, c, "f0", xp, pre + [dtok], True, gwT, uwT, dwT,
                    gs_col=cols["gs"], us_col=cols["us"], dshalf_col=dshalf)
    st = p.do("sync", lambda e: e.dma_start(out=fmview(x_out), in_=xsbview(c.x)),
              waits=toks, sem="sY", by=16)
    p.wait_only("sync", [st] + p.dma_drain_toks())
    p.finalize()
    es.close()
    return nc


def build_l1ffn():
    nc = bass.Bass("TRN2", target_bir_lowering=False)
    x_fm = nc.dram_tensor("x_fm", [D, TC], F32, kind="ExternalInput").ap()
    w1tp = nc.dram_tensor("w1tp", [IFF, D], F32, kind="ExternalInput").ap()
    w3tp = nc.dram_tensor("w3tp", [IFF, D], F32, kind="ExternalInput").ap()
    w2pp = nc.dram_tensor("w2pp", [D, IFF], BF16, kind="ExternalInput").ap()
    h128 = nc.dram_tensor("h128", [128, 128], BF16, kind="ExternalInput").ap()
    x_out = nc.dram_tensor("x_out", [D, TC], F32, kind="ExternalOutput").ap()
    p = P(nc)
    es = ExitStack()
    c = alloc_common(p, es)
    h128s = es.enter_context(nc.sbuf_tensor("l1f_h128", [128, 128], BF16))
    htok = p.do("sync", lambda e: e.dma_start(out=h128s[:, :], in_=h128),
                sem="sY", by=16)
    load_x(p, c, x_fm)
    xp = es.enter_context(nc.sbuf_tensor("l1f_xp", [128, NCH * TC], BF16, side="right"))
    rows, pre = emit_ln_prescale(p, es, c, xp, c.x_tok, "f1")
    toks = emit_ffn(p, es, c, "f1", xp, pre + [htok], False, w1tp, w3tp, w2pp,
                    h128_sb=h128s)
    st = p.do("sync", lambda e: e.dma_start(out=fmview(x_out), in_=xsbview(c.x)),
              waits=toks, sem="sY", by=16)
    p.wait_only("sync", [st] + p.dma_drain_toks())
    p.finalize()
    es.close()
    return nc


def build_l0attnA():
    nc = bass.Bass("TRN2", target_bir_lowering=False)
    x_fm = nc.dram_tensor("x_fm", [D, TC], F32, kind="ExternalInput").ap()
    Wq = nc.dram_tensor("Wq", [D, D], F32, kind="ExternalInput").ap()
    Wk = nc.dram_tensor("Wk", [D, D], F32, kind="ExternalInput").ap()
    Wv = nc.dram_tensor("Wv", [D, D], F32, kind="ExternalInput").ap()
    bq_in = nc.dram_tensor("bq_col", [128, 16], F32, kind="ExternalInput").ap()
    bk_in = nc.dram_tensor("bk_col", [128, 16], F32, kind="ExternalInput").ap()
    bv_in = nc.dram_tensor("bv_col", [128, 16], F32, kind="ExternalInput").ap()
    dr = {}
    dr["k_in"] = nc.dram_tensor("k_out", [D, TC], BF16, kind="ExternalOutput").ap()
    dr["v_in"] = nc.dram_tensor("v_out", [D, TC], BF16, kind="ExternalOutput").ap()
    dr["q_d"] = nc.dram_tensor("q_out", [128, 2 * T], BF16, kind="ExternalOutput").ap()
    p = P(nc)
    es = ExitStack()
    c = alloc_common(p, es)
    cols, ctok = _col_loads(p, nc, es, [
        ("bq", bq_in, [128, 16], F32), ("bk", bk_in, [128, 16], F32),
        ("bv", bv_in, [128, 16], F32)])
    load_x(p, c, x_fm)
    q_int = es.enter_context(nc.sbuf_tensor("a0_qint", [128, 2 * T], BF16,
                                            side="right"))
    xp_es = ExitStack()
    xp = xp_es.enter_context(nc.sbuf_tensor("l0a_xp", [128, NCH * TC], BF16,
                                            side="right"))
    rows, pre = emit_ln_prescale(p, es, c, xp, c.x_tok, "a0")
    W = {"Wq": Wq, "Wk": Wk, "Wv": Wv, "bq_col": cols["bq"],
         "bk_col": cols["bk"], "bv_col": cols["bv"]}
    toks = emit_l0attnA(p, es, c, xp, pre + [ctok], W, dr, q_int)
    p.wait_only("gpsimd", toks + p.dma_drain_toks())
    p.finalize()
    xp_es.close()
    es.close()
    return nc


def build_l0attnB():
    nc = bass.Bass("TRN2", target_bir_lowering=False)
    x_fm = nc.dram_tensor("x_fm", [D, TC], F32, kind="ExternalInput").ap()
    K_in = nc.dram_tensor("K_in", [128, 2 * T], BF16, kind="ExternalInput").ap()
    V_in = nc.dram_tensor("V_in", [128, 2 * T], BF16, kind="ExternalInput").ap()
    Q_in = nc.dram_tensor("Q_in", [128, 2 * T], BF16, kind="ExternalInput").ap()
    wo_my = nc.dram_tensor("wo_my", [256, D], F32, kind="ExternalInput").ap()
    bo_in = nc.dram_tensor("bo_col", [128, 16], F32, kind="ExternalInput").ap()
    m4_in = nc.dram_tensor("mask4", [128, 4 * 512], F32, kind="ExternalInput").ap()
    idb_in = nc.dram_tensor("ident_bf", [128, 128], BF16, kind="ExternalInput").ap()
    x_out = nc.dram_tensor("x_out", [D, TC], F32, kind="ExternalOutput").ap()
    dr = {"K_in": K_in, "V_in": V_in, "Q_in": Q_in}
    dr["rs_in"] = nc.dram_tensor("rs_in", [T, D], BF16, kind="Internal").ap()
    dr["rs_out"] = nc.dram_tensor("rs_out", [TC, D], BF16, kind="Internal").ap()
    p = P(nc)
    es = ExitStack()
    c = alloc_common(p, es)
    cols, ctok = _col_loads(p, nc, es, [
        ("bo", bo_in, [128, 16], F32), ("m4", m4_in, [128, 4 * 512], F32)])
    load_x(p, c, x_fm)
    W = {"wo_my": wo_my, "bo_col": cols["bo"], "ident_bf": idb_in}
    toks = emit_l0attnB(p, es, c, W, dr, cols["m4"], ctok)
    st = p.do("sync", lambda e: e.dma_start(out=fmview(x_out), in_=xsbview(c.x)),
              waits=toks, sem="sY", by=16)
    p.wait_only("sync", [st] + p.dma_drain_toks())
    p.finalize()
    es.close()
    return nc


def build_l1attnA():
    nc = bass.Bass("TRN2", target_bir_lowering=False)
    x_fm = nc.dram_tensor("x_fm", [D, TC], F32, kind="ExternalInput").ap()
    expT = nc.dram_tensor("expandT", [8704, D], F32, kind="ExternalInput").ap()
    ga_in = nc.dram_tensor("gamma_col", [128, 16], F32, kind="ExternalInput").ap()
    be_in = nc.dram_tensor("beta_col", [128, 16], F32, kind="ExternalInput").ap()
    dr = {}
    dr["k2_in"] = nc.dram_tensor("k2_out", [256, TC], BF16, kind="ExternalOutput").ap()
    dr["val_in"] = nc.dram_tensor("val_out", [D, TC], BF16, kind="ExternalOutput").ap()
    dr["gl"] = nc.dram_tensor("gl_out", [D, TC], BF16, kind="ExternalOutput").ap()
    dr["q2_d"] = nc.dram_tensor("q2_out", [128, 2 * TC], BF16, kind="ExternalOutput").ap()
    dr["rows_d"] = nc.dram_tensor("rows_out", [1, 2 * TC], F32, kind="ExternalOutput").ap()
    p = P(nc)
    es = ExitStack()
    c = alloc_common(p, es)
    cols, ctok = _col_loads(p, nc, es, [
        ("gamma", ga_in, [128, 16], F32), ("beta", be_in, [128, 16], F32)])
    load_x(p, c, x_fm)
    W = {"expandT": expT, "gamma_col": cols["gamma"], "beta_col": cols["beta"]}
    toks = emit_l1attnA(p, es, c, [c.x_tok, ctok], W, dr)
    p.wait_only("gpsimd", toks + p.dma_drain_toks())
    p.finalize()
    es.close()
    return nc


def build_l1attnB():
    nc = bass.Bass("TRN2", target_bir_lowering=False)
    x_fm = nc.dram_tensor("x_fm", [D, TC], F32, kind="ExternalInput").ap()
    prjT = nc.dram_tensor("projectT", [D, 4096], F32, kind="ExternalInput").ap()
    k2g = nc.dram_tensor("k2g", [D, TC], BF16, kind="ExternalInput").ap()
    valg = nc.dram_tensor("valg", [NCORE * D, TC], BF16, kind="ExternalInput").ap()
    gl_in = nc.dram_tensor("gl_in", [D, TC], BF16, kind="ExternalInput").ap()
    q2_in = nc.dram_tensor("q2_in", [128, 2 * TC], BF16, kind="ExternalInput").ap()
    rows_in = nc.dram_tensor("rows_in", [1, 2 * TC], F32, kind="ExternalInput").ap()
    pos_in = nc.dram_tensor("pos", [1, 1], F32, kind="ExternalInput").ap()
    rel_in = nc.dram_tensor("relpos", [128, 128], F32, kind="ExternalInput").ap()
    io5_in = nc.dram_tensor("iota512", [1, 512], F32, kind="ExternalInput").ap()
    msq_in = nc.dram_tensor("maskseq", [2 * 64 * 128, 512], BF16,
                            kind="ExternalInput").ap()
    x_out = nc.dram_tensor("x_out", [D, TC], F32, kind="ExternalOutput").ap()
    dr = {"k2g": k2g, "valg": valg, "gl": gl_in, "q2_d": q2_in,
          "rows_d": rows_in, "maskseq": msq_in}
    dr["attn"] = nc.dram_tensor("attn", [D, TC], BF16, kind="Internal").ap()
    p = P(nc)
    es = ExitStack()
    c = alloc_common(p, es)
    cols, ctok = _col_loads(p, nc, es, [
        ("pos", pos_in, [1, 1], F32), ("relpos", rel_in, [128, 128], F32),
        ("iota512", io5_in, [1, 512], F32)])
    load_x(p, c, x_fm)
    W = {"projectT": prjT, "pos_sb": cols["pos"], "relpos": cols["relpos"],
         "iota512": cols["iota512"]}
    toks = emit_l1attnB(p, es, c, ctok, W, dr)
    st = p.do("sync", lambda e: e.dma_start(out=fmview(x_out), in_=xsbview(c.x)),
              waits=toks, sem="sY", by=16)
    p.wait_only("sync", [st] + p.dma_drain_toks())
    p.finalize()
    es.close()
    return nc


_PROGS = {}


def _get_prog(name):
    if name not in _PROGS:
        _PROGS[name] = {"prep": build_prep, "finish": build_finish,
                        "l0ffn": build_l0ffn, "l1ffn": build_l1ffn,
                        "l0attnA": build_l0attnA, "l0attnB": build_l0attnB,
                        "l1attnA": build_l1attnA, "l1attnB": build_l1attnB}[name]()
    return _PROGS[name]


def _slabify(wT):
    K, M = wT.shape
    nk, no = K // 128, M // 128
    return np.ascontiguousarray(
        wT.reshape(nk, 128, no, 128).transpose(2, 1, 0, 3).reshape(M, K))


def _had(n):
    H = np.ones((1, 1), np.float32)
    while H.shape[0] < n:
        H = np.block([[H, H], [H, -H]])
    return H


_HOST = {}


def _host_prep(inputs):
    import ml_dtypes
    H = {}
    ct = np.ascontiguousarray
    H["gwT"] = _slabify(inputs["L0_ffn_gate_w"].T)
    H["uwT"] = _slabify(inputs["L0_ffn_up_w"].T)
    H["dwT"] = _slabify(inputs["L0_ffn_down_w"].T)
    H["gs_col"] = ct(inputs["L0_ffn_gate_scale"].reshape(64, 128).T)
    H["us_col"] = ct(inputs["L0_ffn_up_scale"].reshape(64, 128).T)
    H["ds_col"] = ct(inputs["L0_ffn_down_scale"].reshape(16, 128).T)
    for nm in ("Wq", "Wk", "Wv", "Wo"):
        H[nm] = _slabify(inputs["L0_attn_W" + nm[-1].lower()])
    for nm in ("bq", "bk", "bv"):
        H[nm + "_col"] = ct(inputs["L0_attn_" + nm].reshape(16, 128).T)
    H["bo_row"] = ct(inputs["L0_attn_bo"].reshape(1, D))
    W1T = inputs["L1_ffn_W1"].T
    W3T = inputs["L1_ffn_W3"].T
    H["w1tp"] = _slabify(W1T.reshape(D, 128, 64).transpose(0, 2, 1).reshape(D, IFF))
    H["w3tp"] = _slabify(W3T.reshape(D, 128, 64).transpose(0, 2, 1).reshape(D, IFF))
    H["w2t"] = ct(inputs["L1_ffn_W2"].T)
    H["expandT"] = _slabify(inputs["L1_attn_expand"].T)
    H["projectT"] = _slabify(inputs["L1_attn_project"].T)
    H["gamma_col"] = ct(inputs["L1_attn_ln_gamma"].reshape(16, 128).T)
    H["beta_col"] = ct(inputs["L1_attn_ln_beta"].reshape(16, 128).T)
    H["pos"] = np.asarray(inputs["L1_attn_pos_mult"], np.float32).reshape(1, 1)
    H["h64"] = _had(64).astype(ml_dtypes.bfloat16)
    H["h128"] = _had(128).astype(ml_dtypes.bfloat16)
    H["ident"] = np.eye(128, dtype=np.float32)
    H["ident_bf"] = np.eye(128, dtype=ml_dtypes.bfloat16)
    H["wo_my"] = [ct(inputs["L0_attn_Wo"][256 * cc:256 * (cc + 1), :])
                  for cc in range(NCORE)]
    H["bo_col"] = ct(inputs["L0_attn_bo"].reshape(16, 128).T)
    H["iota512"] = np.arange(512, dtype=np.float32).reshape(1, 512)
    pm = np.arange(128)[:, None]
    nm_ = np.arange(512)[None, :]
    m4 = np.concatenate(
        [np.where(128 * d + pm > nm_, np.float32(-1e30), np.float32(0.0))
         for d in range(4)], axis=1).astype(np.float32)
    H["mask4"] = ct(m4)
    x = inputs["L0_attn_x"].reshape(T, D)
    H["x_tok"] = [ct(x[cc * TC:(cc + 1) * TC]) for cc in range(NCORE)]
    H["coreoff"] = [np.array([[256 * cc, 1024 * cc]], np.int32)
                    for cc in range(NCORE)]
    rel = []
    msq = []
    for cc_core in range(NCORE):
        r = np.zeros((128, 128), np.float32)
        m = np.zeros((2 * 64 * 128, 512), np.float32)
        for g in range(2):
            qlo = TC * cc_core + 512 * g
            for ck in range(64):
                r[:, g * 64 + ck] = 128 * ck + np.arange(128) - qlo
                s_idx = 128 * ck + np.arange(128)[:, None]
                q_idx = qlo + np.arange(512)[None, :]
                m[(g * 64 + ck) * 128:(g * 64 + ck + 1) * 128, :] = np.where(
                    s_idx <= q_idx, np.float32(0.0), np.float32(-1e30))
        rel.append(r)
        msq.append(m.astype(ml_dtypes.bfloat16))
    H["relpos"] = rel
    H["maskseq"] = msq
    return H


def _run(name, in_maps):
    nc = _get_prog(name)
    res = run_bass_kernel_spmd(nc, in_maps, core_ids=list(range(NCORE)))
    return res.results


def kernel(**inputs):
    H = _host_prep(inputs)
    cores = range(NCORE)
    r = _run("prep", [{"x_tok": H["x_tok"][i], "w2t": H["w2t"], "h64": H["h64"],
                       "ident": H["ident"]} for i in cores])
    xfm = [r[i]["x_fm"] for i in cores]
    w2pp = [_slabify(r[i]["w2pp"]) for i in cores]

    def ffn0(xfm):
        r = _run("l0ffn", [{"x_fm": xfm[i], "gwT": H["gwT"], "uwT": H["uwT"],
                            "dwT": H["dwT"], "gs_col": H["gs_col"],
                            "us_col": H["us_col"], "ds_col": H["ds_col"]}
                           for i in cores])
        return [r[i]["x_out"] for i in cores]

    def attn0(xfm):
        ra = _run("l0attnA", [{"x_fm": xfm[i], "Wq": H["Wq"], "Wk": H["Wk"],
                               "Wv": H["Wv"], "bq_col": H["bq_col"],
                               "bk_col": H["bk_col"], "bv_col": H["bv_col"]}
                              for i in cores])
        k_full = np.stack([ra[i]["k_out"] for i in cores])
        v_full = np.stack([ra[i]["v_out"] for i in cores])
        p_idx = np.arange(128)
        Kin, Vin = [], []
        for ccore in range(NCORE):
            Kc = np.empty((128, 2 * T), k_full.dtype)
            Vc = np.empty((128, 2 * T), v_full.dtype)
            for h in range(2):
                for r_ in range(NCORE):
                    Kc[:, h * T + r_ * TC:h * T + (r_ + 1) * TC] = \
                        k_full[r_, 256 * ccore + 128 * h:256 * ccore + 128 * (h + 1), :]
                for cc2 in range(64):
                    r_, j = cc2 // 8, cc2 % 8
                    Vc[:, (h * 64 + cc2) * 128:(h * 64 + cc2 + 1) * 128] = \
                        v_full[r_, 256 * ccore + 128 * h:256 * ccore + 128 * (h + 1),
                               j * 128:(j + 1) * 128].T
            Kin.append(Kc)
            Vin.append(Vc)
        rb = _run("l0attnB", [{"x_fm": xfm[i], "K_in": Kin[i], "V_in": Vin[i],
                               "Q_in": ra[i]["q_out"], "wo_my": H["wo_my"][i],
                               "bo_col": H["bo_col"], "mask4": H["mask4"],
                               "ident_bf": H["ident_bf"]} for i in cores])
        return [rb[i]["x_out"] for i in cores]

    def ffn1(xfm):
        r = _run("l1ffn", [{"x_fm": xfm[i], "w1tp": H["w1tp"], "w3tp": H["w3tp"],
                            "w2pp": w2pp[i], "h128": H["h128"]} for i in cores])
        return [r[i]["x_out"] for i in cores]

    def attn1(xfm):
        ra = _run("l1attnA", [{"x_fm": xfm[i], "expandT": H["expandT"],
                               "gamma_col": H["gamma_col"],
                               "beta_col": H["beta_col"]} for i in cores])
        k2g = np.concatenate([ra[i]["k2_out"] for i in cores], axis=0)
        valg = np.concatenate([ra[i]["val_out"] for i in cores], axis=0)
        rb = _run("l1attnB", [{"x_fm": xfm[i], "projectT": H["projectT"],
                               "k2g": k2g, "valg": valg, "gl_in": ra[i]["gl_out"],
                               "q2_in": ra[i]["q2_out"], "rows_in": ra[i]["rows_out"],
                               "pos": H["pos"], "relpos": H["relpos"][i],
                               "iota512": H["iota512"],
                               "maskseq": H["maskseq"][i]} for i in cores])
        return [rb[i]["x_out"] for i in cores]

    xfm = ffn0(xfm)
    xfm = attn0(xfm)
    xfm = ffn0(xfm)
    xfm = ffn1(xfm)
    xfm = attn1(xfm)
    xfm = ffn1(xfm)
    r = _run("finish", [{"x_fm": xfm[i], "ident": H["ident"]} for i in cores])
    out = np.concatenate([r[i]["x_tok"] for i in cores], axis=0)
    return out.reshape(1, T, D)
